# version 26
# baseline (speedup 1.0000x reference)
import numpy as np
import ml_dtypes
import concourse.bass as bass
import concourse.mybir as mybir
from concourse.bass_utils import run_bass_kernel_spmd

F32 = mybir.dt.float32
BF16 = mybir.dt.bfloat16
I32 = mybir.dt.int32
AF = mybir.ActivationFunctionType
ALU = mybir.AluOpType
AX = mybir.AxisListType

T = 4096
C = 1024
H3 = 3072
NT = T // 128
NCH = C // 128
NP = 8
NW = T // 512
NHT = H3 // 128
NGRP = 4
GHT = NHT // NGRP
INV127 = 1.0 / 127.0
INV127SQ = 1.0 / (127.0 * 127.0)

ENGS = ("T", "V", "S", "Y", "G")


class Sem:
    def __init__(self, nc, name):
        self._cm = nc.semaphore(name)
        self.h = self._cm.__enter__()
        self.n = 0


class Builder:

    def __init__(self, nc):
        self.nc = nc
        self.streams = {e: [] for e in ENGS}
        self.known = {e: {} for e in ENGS}
        self.own = {}

    def op(self, e, fn, waits=(), inc=(), dma_inc=()):
        known = self.known[e]
        waits = list(waits)
        own = self.own.get(e)
        if own is not None:
            waits.append((own, own.n))
        wl = []
        for (s, v) in waits:
            if v <= 0 or known.get(s, -1) >= v:
                continue
            wl.append((s.h, v))
            known[s] = max(known.get(s, -1), v)
        post = {}
        pairs = []
        for (s, n) in inc:
            s.n += n
            pairs.append((s.h, n))
            post[s] = s.n
        for (s, n) in dma_inc:
            s.n += n
            pairs.append((s.h, n))
            post[s] = s.n

        def run(eng, wl=wl, pairs=pairs, fn=fn):
            for (h, v) in wl:
                eng.wait_ge(h, v)
            inst = fn(eng)
            for (h, n) in pairs:
                inst = inst.then_inc(h, n)

        self.streams[e].append(run)
        return post


def build_nc(debug=False):
    nc = bass.Bass("TRN2", target_bir_lowering=False,
                   dynamic_dma_scratch_size=16384)
    ikind = "ExternalOutput" if debug else "Internal"

    x_d = nc.dram_tensor("x", [T, C], F32, kind="ExternalInput").ap()
    w_d = nc.dram_tensor("w", [C, H3], F32, kind="ExternalInput").ap()
    bpt_d = nc.dram_tensor("bpt_in", [128, 24], F32, kind="ExternalInput").ap()
    brow_d = nc.dram_tensor("brow_in", [1, 1024], F32, kind="ExternalInput").ap()
    wg_d = nc.dram_tensor("wg", [H3, C], I32, kind="ExternalInput").ap()
    wu_d = nc.dram_tensor("wu", [H3, C], I32, kind="ExternalInput").ap()
    wd_d = nc.dram_tensor("wd", [C, H3], I32, kind="ExternalInput").ap()
    identb_d = nc.dram_tensor("identb_in", [128, 128], BF16, kind="ExternalInput").ap()
    masks_d = nc.dram_tensor("masks", [4, 128, 512], BF16, kind="ExternalInput").ap()
    out_d = nc.dram_tensor("out", [T, C], F32, kind="ExternalOutput").ap()

    qTh = nc.dram_tensor("qTh", [C, T], BF16, kind=ikind).ap()
    kTh = nc.dram_tensor("kTh", [C, T], BF16, kind=ikind).ap()
    vh = nc.dram_tensor("vh", [128, NT, C], BF16, kind=ikind).ap()

    big0 = nc.alloc_sbuf_tensor("big0", [128, NCH, T], BF16).ap()
    big1 = nc.alloc_sbuf_tensor("big1", [128, 32768], BF16).ap()
    big2 = nc.alloc_sbuf_tensor("big2", [128, 15488], BF16).ap()
    hTt = nc.alloc_sbuf_tensor("hTt", [128, 3, T], BF16).ap()
    wdt = nc.alloc_sbuf_tensor("wdt", [128, 3, 1024], BF16).ap()
    xnstage = nc.alloc_sbuf_tensor("xnstage", [128, 1024], BF16).ap()
    wstage = nc.alloc_sbuf_tensor("wstage", [128, 2, 1024], BF16).ap()
    wgt_t = nc.alloc_sbuf_tensor("wgt_t", [128, 1024], BF16).ap()
    wut_t = nc.alloc_sbuf_tensor("wut_t", [128, 1024], BF16).ap()
    gs = nc.alloc_sbuf_tensor("gs", [128, 512], BF16).ap()
    masksb = nc.alloc_sbuf_tensor("masksb", [128, 4, 512], BF16).ap()
    identb = nc.alloc_sbuf_tensor("identb", [128, 128], BF16).ap()
    ones128 = nc.alloc_sbuf_tensor("ones128", [128, 1], BF16).ap()
    ones1 = nc.alloc_sbuf_tensor("ones1", [1, 128], BF16).ap()
    bpt = nc.alloc_sbuf_tensor("bpt", [128, 24], F32).ap()
    browb = nc.alloc_sbuf_tensor("browb", [1, 1024], BF16).ap()
    stats = nc.alloc_sbuf_tensor("stats", [128, 48], F32).ap()
    epsb = nc.alloc_sbuf_tensor("epsb", [128, 1], F32).ap()
    nstat = nc.alloc_sbuf_tensor("nstat", [1, 32], F32).ap()
    negb11 = nc.alloc_sbuf_tensor("negb11", [1, 1], BF16).ap()
    negB = nc.alloc_sbuf_tensor("negB", [128, 1], F32).ap()
    rl = nc.alloc_sbuf_tensor("rl", [128, 8], F32).ap()
    ptsb = nc.alloc_sbuf_tensor("ptsb", [128, 544], BF16).ap()
    ptmp = nc.alloc_sbuf_tensor("ptmp", [128, 64], BF16).ap()

    acc = big1[:, 0:32768].rearrange("p (t c) -> p t c", t=NT)
    Wv = big2[:, 0:12288].rearrange("p (c n) -> p c n", c=NCH)
    qkstage = big2[:, 12288:14336].rearrange("p (r n) -> p r n", r=4)
    kqv = big2[:, 0:8192].rearrange("p (i t) -> p i t", i=2)
    vbufraw = big2[:, 8192:12416]
    vbuf = vbufraw[:, 0:4160].rearrange("p (c n) -> p c n", c=NT)
    pTv = big2[:, 12416:15488].rearrange("p (i n) -> p i n", i=3)
    ovst = xnstage[:, 0:1024]
    smxA = hTt[:, 0:3, :].rearrange("p a t -> p (a t)").bitcast(F32) \
        .rearrange("p (i n) -> p i n", i=6)
    outstF = big2[:, 4096:8192].bitcast(F32).rearrange("p (i n) -> p i n", i=2)

    psA = nc.alloc_psum_tensor("psA", [128, 8, 512], F32).ap()

    def bank(i):
        return psA[:, i, :]

    def wide(i):
        return psA[:, i:i + 2, :].rearrange("p a b -> p (a b)")

    def bank_bf(i):
        return psA[:, i, :].bitcast(BF16)

    B = Builder(nc)
    qC = Sem(nc, "qC")
    qX = [Sem(nc, "qX%d" % i) for i in range(6)]
    qKq = [Sem(nc, "qK0"), Sem(nc, "qK1")]
    qV = Sem(nc, "qV")
    qW = Sem(nc, "qW")
    qWS = [Sem(nc, "qWS0"), Sem(nc, "qWS1")]
    qST = [Sem(nc, "qS0"), Sem(nc, "qS1"), Sem(nc, "qS2"), Sem(nc, "qS3")]
    qOU = [Sem(nc, "qO0"), Sem(nc, "qO1")]
    sT = Sem(nc, "sT")
    sV = Sem(nc, "sV")
    sS = Sem(nc, "sS")
    sG = Sem(nc, "sG")
    B.own = {"V": sV, "S": sS, "G": sG}

    B.op("Y", lambda e: e.dma_start(out=identb[:, :], in_=identb_d[:, :]), dma_inc=[(qC, 16)])
    for j in range(4):
        dst = masksb[:, j, :]
        src = masks_d[j, :, :]
        B.op("Y", lambda e, dst=dst, src=src: e.dma_start(out=dst, in_=src), dma_inc=[(qC, 16)])
    B.op("Y", lambda e: e.dma_start(out=bpt[:, :], in_=bpt_d[:, :]), dma_inc=[(qC, 16)])
    B.op("G", lambda e: e.dma_start(out=browb[:, :], in_=brow_d[:, :]), dma_inc=[(qC, 16)])
    const_ld = qC.n
    B.op("G", lambda e: e.memset(ones128[:, :], 1.0), inc=[(sG, 1)])
    B.op("G", lambda e: e.memset(ones1[:, :], 1.0), inc=[(sG, 1)])
    B.op("G", lambda e: e.memset(epsb[:, :], 1e-5), inc=[(sG, 1)])
    const_ms = sG.n


    xnslot = [xnstage[:, 0:1024], wstage[:, 1, :]]
    a_apply_v = [0] * NT
    a_sq_s = [0] * NT
    a_tr_t = [0] * NT
    a_evac_v = [0] * NT
    a_xld = [0] * NT

    a_copy_g = [0] * NT
    a_sqrt_s = [0] * NT

    def a_stage1(t):
        sl = t % 6
        o = sl * 8
        xs = smxA[:, sl, :]
        xsrc = x_d[t * 128:(t + 1) * 128, :]
        waits = []
        if t >= 6:
            waits = [(sV, a_apply_v[t - 6]), (sG, a_copy_g[t - 6]),
                     (sS, a_sq_s[t - 6])]
        B.op("Y", lambda e, xs=xs, xsrc=xsrc: e.dma_start(out=xs, in_=xsrc),
             waits=waits, dma_inc=[(qX[sl], 16)])
        a_xld[t] = qX[sl].n

        p = B.op("G", lambda e, d=acc[:, t, :], xs=xs: e.tensor_copy(d, xs),
                 waits=[(qX[sl], a_xld[t])], inc=[(sG, 1)])
        a_copy_g[t] = p[sG]

        p = B.op("S", lambda e, xs=xs, d=pTv[:, 0, :], a=stats[:, o + 2:o + 3]:
                 e.activation(d, xs, AF.Square, accum_out=a),
                 waits=[(qX[sl], a_xld[t])], inc=[(sS, 1)])
        a_sq_s[t] = p[sS]
        B.op("V", lambda e, xs=xs, d=stats[:, o + 0:o + 1]: e.reduce_sum(d, xs, axis=AX.X),
             waits=[(qX[sl], a_xld[t])], inc=[(sV, 1)])
        B.op("V", lambda e, s=stats[:, o + 0:o + 1], d=stats[:, o + 1:o + 2]:
             e.tensor_scalar(d, s, 1.0 / C, None, ALU.mult), inc=[(sV, 1)])
        B.op("V", lambda e, m=stats[:, o + 1:o + 2], d=stats[:, o + 3:o + 4]:
             e.tensor_mul(d, m, m), inc=[(sV, 1)])
        B.op("V", lambda e, ssq=stats[:, o + 2:o + 3], msq=stats[:, o + 3:o + 4],
             d=stats[:, o + 4:o + 5]:
             e.tensor_scalar(d, ssq, 1.0 / C, msq, ALU.mult, ALU.subtract),
             waits=[(sS, a_sq_s[t])], inc=[(sV, 1)])
        pv = sV.n
        p = B.op("S", lambda e, v=stats[:, o + 4:o + 5], d=stats[:, o + 5:o + 6]:
                 e.activation(d, v, AF.Sqrt, bias=epsb[:, 0:1]),
                 waits=[(sV, pv), (sG, const_ms)], inc=[(sS, 1)])
        a_sqrt_s[t] = p[sS]

    def a_stage2(t):
        sl = t % 6
        o = sl * 8
        xs = smxA[:, sl, :]
        B.op("V", lambda e, s=stats[:, o + 5:o + 6], d=stats[:, o + 6:o + 7]:
             e.reciprocal(d, s), waits=[(sS, a_sqrt_s[t])], inc=[(sV, 1)])
        waits = []
        if t >= 2:
            waits.append((sT, a_tr_t[t - 2]))
        p = B.op("V", lambda e, xs=xs, m=stats[:, o + 1:o + 2], r=stats[:, o + 6:o + 7],
                 d=xnslot[t % 2]:
                 e.tensor_scalar(d, xs, m, r, ALU.subtract, ALU.mult),
                 waits=waits, inc=[(sV, 1)])
        a_apply_v[t] = p[sV]
        tb = bank_bf(6 + t % 2)
        waits = [(sV, a_apply_v[t]), (qC, const_ld)]
        if t >= 2:
            waits.append((sV, a_evac_v[t - 2]))
        for c in range(NCH):
            dst = tb[:, c * 128:(c + 1) * 128]
            src = xnslot[t % 2][:, c * 128:(c + 1) * 128]
            p = B.op("T", lambda e, dst=dst, src=src: e.transpose(dst, src, identb[:, :]),
                     waits=waits, inc=[(sT, 1)])
            waits = []
        a_tr_t[t] = p[sT]
        dstx = big0[:, :, t * 128:(t + 1) * 128]
        srcx = tb.rearrange("p (c w) -> p c w", c=NCH)
        p = B.op("V", lambda e, dstx=dstx, srcx=srcx: e.tensor_copy(dstx, srcx),
                 waits=[(sT, a_tr_t[t])], inc=[(sV, 1)])
        a_evac_v[t] = p[sV]

    for t in range(NT):
        a_stage1(t)
        if t >= 1:
            a_stage2(t - 1)
    a_stage2(NT - 1)
    a_end_v = sV.n
    a_end_g = sG.n
    a_end_s = sS.n
    if debug:
        for c in range(NCH):
            dst = xnTd[c]
            src = big0[:, c, :]
            B.op("S", lambda e, dst=dst, src=src: e.dma_start(out=dst, in_=src),
                 waits=[(sV, a_end_v)], dma_inc=[(qC, 16)])

    ft_store_done = {}
    b_cast = [0, 0]
    qk_bank_evac = {}
    stage_dma = [0, 0, 0, 0]
    bmm_i = 0

    for half in range(2):
        waits = [] if half == 0 else [(sT, sT.n)]
        for c in range(NCH):
            dst = Wv[:, c, :]
            src = w_d[c * 128:(c + 1) * 128, half * 1536:(half + 1) * 1536]
            B.op("G", lambda e, dst=dst, src=src: e.dma_start(out=dst, in_=src),
                 waits=waits, dma_inc=[(qW, 16)])
            waits = []
        b_cast[half] = qW.n

        qk_tiles = list(range(12)) if half == 0 else list(range(4))
        for fl in qk_tiles:
            ft = half * 12 + fl
            for tw in range(NW):
                bb2 = bmm_i % 2
                bmm_i += 1
                for c in range(NCH):
                    lhsT = Wv[:, c, fl * 128:(fl + 1) * 128]
                    rhs = big0[:, c, tw * 512:(tw + 1) * 512]
                    waits = [(qW, b_cast[half]), (sV, a_end_v)]
                    if c == 0 and bb2 in qk_bank_evac:
                        waits.append((sV, qk_bank_evac[bb2]))
                    B.op("T", lambda e, lhsT=lhsT, rhs=rhs, o=bank(bb2), c=c:
                         e.matmul(o, lhsT, rhs, start=(c == 0), stop=(c == NCH - 1)),
                         waits=waits, inc=[(sT, 1)])
                mm_done = sT.n
                r = (ft * NW + tw) % 4
                p = B.op("V", lambda e, d=qkstage[:, r, :], s=bank(bb2), bc=bpt[:, ft:ft + 1]:
                         e.tensor_scalar(d, s, bc, None, ALU.add),
                         waits=[(sT, mm_done), (qC, const_ld), (qST[r], stage_dma[r])],
                         inc=[(sV, 1)])
                qk_bank_evac[bb2] = p[sV]
                ev = p[sV]
                if ft < 8:
                    dst = qTh[ft * 128:(ft + 1) * 128, tw * 512:(tw + 1) * 512]
                else:
                    dst = kTh[(ft - 8) * 128:(ft - 8 + 1) * 128, tw * 512:(tw + 1) * 512]
                p = B.op("S", lambda e, dst=dst, s=qkstage[:, r, :]: e.dma_start(out=dst, in_=s),
                         waits=[(sV, ev)], dma_inc=[(qST[r], 16)])
                stage_dma[r] = p[qST[r]]
                if tw == NW - 1:
                    ft_store_done[ft] = [(qST[rr], qST[rr].n) for rr in range(4)]

    v_bank_evac = {}
    for tt in range(NT):
        for vc in range(2):
            bb2 = 2 + (tt * 2 + vc) % 2
            for c in range(NCH):
                lhsT = big0[:, c, tt * 128:(tt + 1) * 128]
                rhs = Wv[:, c, 512 + vc * 512:512 + (vc + 1) * 512]
                waits = [(qW, b_cast[1])]
                if c == 0 and bb2 in v_bank_evac:
                    waits.append((sV, v_bank_evac[bb2]))
                B.op("T", lambda e, lhsT=lhsT, rhs=rhs, o=bank(bb2), c=c:
                     e.matmul(o, lhsT, rhs, start=(c == 0), stop=False),
                     waits=waits, inc=[(sT, 1)])
            B.op("T", lambda e, o=bank(bb2), r=browb[0:1, vc * 512:(vc + 1) * 512]:
                 e.matmul(o, ones1[:, :], r, start=False, stop=True),
                 waits=[(qC, const_ld), (sG, const_ms)], inc=[(sT, 1)])
            mm_done = sT.n
            r = (tt * 2 + vc) % 4
            p = B.op("V", lambda e, d=qkstage[:, r, :], s=bank(bb2):
                     e.tensor_copy(d, s),
                     waits=[(sT, mm_done), (qST[r], stage_dma[r])], inc=[(sV, 1)])
            v_bank_evac[bb2] = p[sV]
            ev = p[sV]
            dst = vh[:, tt, vc * 512:(vc + 1) * 512]
            p = B.op("S", lambda e, dst=dst, s=qkstage[:, r, :]: e.dma_start(out=dst, in_=s),
                     waits=[(sV, ev)], dma_inc=[(qST[r], 16)])
            stage_dma[r] = p[qST[r]]
    b_end_t = sT.n
    b_end_v = sV.n
    b_stores = [(qST[r], qST[r].n) for r in range(4)]

    acc_zero_v = a_end_v

    B.op("G", lambda e: e.memset(vbuf[:, :, 64:65], 1.0),
         waits=[(sT, b_end_t)], inc=[(sG, 1)])
    B.op("G", lambda e: e.memset(vbuf[:, :, 129:130], 1.0), inc=[(sG, 1)])
    B.op("G", lambda e: e.memset(vbufraw[:, 4160:4224], 0.0), inc=[(sG, 1)])
    vones_ms = sG.n

    b67 = {6: {"w": b_end_t, "r": []}, 7: {"w": b_end_t, "r": []}}

    def bank_w(b, waits=()):
        return list(waits) + b67[b]["r"]

    def bank_w_done(b):
        b67[b]["w"] = sT.n
        b67[b]["r"] = []

    def bank_r(b, waits=()):
        return list(waits) + [(sT, b67[b]["w"])]

    def bank_r_done(b, s, v):
        b67[b]["r"].append((s, v))

    ds = {
        "wstage_tr": [b_end_t, b_end_t],
        "wgt_ev": acc_zero_v, "wut_ev": acc_zero_v,
        "silu": 0, "hmul": 0, "uev": 0,
        "gu_done": b_end_t,
        "down_t": b_end_t,
        "h_ready": 0, "wdt_ready": 0,
        "stt_last": acc_zero_v,
    }
    DG = 8
    DGH = 3
    wd_r = wd_d.rearrange("(dc p) h -> p dc h", p=128)

    def d_jit_gu(ht):
        for (slot, wsrc, dstt, evk) in ((0, wg_d, wgt_t, "wgt_ev"), (1, wu_d, wut_t, "wut_ev")):
            waits = [(sT, ds["wstage_tr"][slot])]
            dst = wstage[:, slot, :]
            srcw = wsrc[ht * 128:(ht + 1) * 128, :]
            B.op("G", lambda e, dst=dst, srcw=srcw: e.dma_start(out=dst, in_=srcw),
                 waits=waits, dma_inc=[(qWS[slot], 16)])
            cast = qWS[slot].n
            tb = 6 + slot
            tbf = bank_bf(tb)
            waits = bank_w(tb, [(qWS[slot], cast), (sV, ds[evk])])
            for dc in range(NCH):
                dstp = tbf[:, dc * 128:(dc + 1) * 128]
                srcp = wstage[:, slot, dc * 128:(dc + 1) * 128]
                B.op("T", lambda e, dstp=dstp, srcp=srcp:
                     e.transpose(dstp, srcp, identb[:, :]),
                     waits=waits, inc=[(sT, 1)])
                waits = []
            ds["wstage_tr"][slot] = sT.n
            bank_w_done(tb)
            trd = sT.n
            p = B.op("V", lambda e, d=dstt, s=tbf: e.tensor_copy(d, s),
                     waits=[(sT, trd)], inc=[(sV, 1)])
            bank_r_done(tb, sV, p[sV])
            ds[evk] = p[sV]

    def d_guh(g, j3, tw):
        gb = 6 + (tw % 2)
        ub = 6 + ((tw + 1) % 2)
        for (tb, wt, evk) in ((gb, wgt_t, "wgt_ev"), (ub, wut_t, "wut_ev")):
            waits = bank_w(tb, [(sV, ds[evk])])
            if tw == 0 and j3 == 0:
                waits.append((sT, ds["down_t"]))
            for c in range(NCH):
                lhsT = wt[:, c * 128:(c + 1) * 128]
                rhs = big0[:, c, tw * 512:(tw + 1) * 512]
                B.op("T", lambda e, o=bank(tb), lhsT=lhsT, rhs=rhs, c=c:
                     e.matmul(o, lhsT, rhs, start=(c == 0), stop=(c == NCH - 1)),
                     waits=waits, inc=[(sT, 1)])
                waits = []
            bank_w_done(tb)
        ds["gu_done"] = sT.n
        gu_done = sT.n
        waits = bank_r(gb, [(sV, ds["hmul"])])
        p = B.op("S", lambda e, s=bank(gb): e.activation(gs[:, :], s, AF.Silu,
                                                         scale=INV127),
                 waits=waits, inc=[(sS, 1)])
        ds["silu"] = p[sS]
        bank_r_done(gb, sS, p[sS])
        dsth = hTt[:, j3, tw * 512:(tw + 1) * 512]
        waits = bank_r(ub, [(sS, ds["silu"]), (sG, a_end_g), (sS, a_end_s)])
        p = B.op("V", lambda e, d=dsth, s=bank(ub): e.tensor_tensor(d, gs[:, :], s, ALU.mult),
                 waits=waits, inc=[(sV, 1)])
        ds["hmul"] = p[sV]
        bank_r_done(ub, sV, p[sV])
        if j3 == DGH - 1 and tw == NW - 1:
            ds["h_ready"] = p[sV]

    def d_wd_jit(g, j3):
        hc = g * DGH + j3
        slot = j3 % 2
        waits = [(sT, ds["wstage_tr"][slot])]
        for dcc in range(NCH):
            dst = wstage[:, slot, dcc * 128:(dcc + 1) * 128]
            srcw = wd_r[:, dcc, hc * 128:(hc + 1) * 128]
            B.op("G", lambda e, dst=dst, srcw=srcw: e.dma_start(out=dst, in_=srcw),
                 waits=waits, dma_inc=[(qWS[slot], 16)])
            waits = []
        cast = qWS[slot].n
        tb = 6 + slot
        tbf = bank_bf(tb)
        waits = bank_w(tb, [(qWS[slot], cast), (sT, ds["down_t"])])
        for dc in range(NCH):
            dstp = tbf[:, dc * 128:(dc + 1) * 128]
            srcp = wstage[:, slot, dc * 128:(dc + 1) * 128]
            B.op("T", lambda e, dstp=dstp, srcp=srcp:
                 e.transpose(dstp, srcp, identb[:, :]),
                 waits=waits, inc=[(sT, 1)])
            waits = []
        ds["wstage_tr"][slot] = sT.n
        bank_w_done(tb)
        trd = sT.n
        p = B.op("V", lambda e, d=wdt[:, j3, :], s=tbf: e.tensor_copy(d, s),
                 waits=[(sT, trd)], inc=[(sV, 1)])
        bank_r_done(tb, sV, p[sV])
        if j3 == DGH - 1:
            ds["wdt_ready"] = p[sV]

    def d_down(g, tt, last_grp_tt):
        for dc in range(2):
            tb = 6 + dc
            waits = bank_w(tb, [(sV, ds["wdt_ready"]), (sV, ds["h_ready"])])
            for j3 in range(DGH):
                lhsT = hTt[:, j3, tt * 128:(tt + 1) * 128]
                rhs = wdt[:, j3, dc * 512:(dc + 1) * 512]
                B.op("T", lambda e, o=bank(tb), lhsT=lhsT, rhs=rhs, j3=j3:
                     e.matmul(o, lhsT, rhs, start=(j3 == 0), stop=(j3 == DGH - 1)),
                     waits=waits, inc=[(sT, 1)])
                waits = []
            bank_w_done(tb)
            dd = sT.n
            dstp = acc[:, tt, dc * 512:(dc + 1) * 512]
            p = B.op("V", lambda e, d=dstp, s=bank(tb):
                     e.scalar_tensor_tensor(d, s, INV127SQ, d, ALU.mult, ALU.add),
                     waits=[(sT, dd), (sG, a_end_g)], inc=[(sV, 1)])
            ds["stt_last"] = p[sV]
            bank_r_done(tb, sV, p[sV])
        if last_grp_tt:
            ds["down_t"] = sT.n
        if g == DG - 1:
            tail["fin_v"][tt] = sV.n

    tail = {"attn_t": None, "fin_v": {},
            "add_v": {}, "outst_dma": [qOU[0].n, qOU[1].n]}

    def emit_tail(tt):
        sl = tt % 2
        p = B.op("V", lambda e, d=outstF[:, sl, :], b=acc[:, tt, :]:
                 e.tensor_copy(d, b),
                 waits=[(sT, tail["attn_t"]),
                        (sV, tail["fin_v"][tt]), (qOU[sl], tail["outst_dma"][sl])],
                 inc=[(sV, 1)])
        tail["add_v"][tt] = p[sV]
        dsto = out_d[tt * 128:(tt + 1) * 128, :]
        p = B.op("S", lambda e, dsto=dsto, s=outstF[:, sl, :]: e.dma_start(out=dsto, in_=s),
                 waits=[(sV, tail["add_v"][tt])], dma_inc=[(qOU[sl], 16)])
        tail["outst_dma"][sl] = p[qOU[sl]]

    dq = []
    for g in range(DG):
        for j3 in range(DGH):
            ht = g * DGH + j3
            dq.append((10, lambda ht=ht: d_jit_gu(ht)))
            for tw in range(NW):
                dq.append((16, lambda g=g, j3=j3, tw=tw: d_guh(g, j3, tw)))
        for j3 in range(DGH):
            dq.append((9, lambda g=g, j3=j3: d_wd_jit(g, j3)))
        for tt in range(NT):
            dq.append((6, lambda g=g, tt=tt: d_down(g, tt, tt == NT - 1)))

    d_total = sum(c for c, _ in dq)
    c_total = NP * sum(4 * (w + 1) for w in range(NW)) * 4
    d_state = {"credit": 0.0, "idx": 0}
    import os
    _NO_ILV = bool(int(os.environ.get("NO_ILV", "0")))
    _SKIP_C = bool(int(os.environ.get("SKIP_C", "0")))
    _SKIP_D = bool(int(os.environ.get("SKIP_D", "0")))
    if _SKIP_D:
        dq = []

    def advance_d(units):
        if _NO_ILV:
            return
        d_state["credit"] += units * (d_total / c_total)
        while d_state["idx"] < len(dq) and d_state["credit"] >= dq[d_state["idx"]][0]:
            d_state["credit"] -= dq[d_state["idx"]][0]
            dq[d_state["idx"]][1]()
            d_state["idx"] += 1

    kq_ld = 0
    kq_last_t = b_end_t
    normsq_last_v = acc_zero_v
    v_ld = 0
    pv_last_t = b_end_t
    exp_done = {}
    pT_rd = {}
    o_free_v = b_end_v
    ov_evac_v = b_end_v
    acc_ready = acc_zero_v

    def load_pair(pr):
        waits = list(ft_store_done[pr]) + list(ft_store_done[8 + pr]) + \
                [(sT, kq_last_t), (sV, normsq_last_v)]
        dst = kqv[:, 0, :]
        srck = kTh[pr * 128:(pr + 1) * 128, :]
        B.op("Y", lambda e, dst=dst, srck=srck: e.dma_start(out=dst, in_=srck),
             waits=waits, dma_inc=[(qKq[0], 16)])
        dst = kqv[:, 1, :]
        srcq = qTh[pr * 128:(pr + 1) * 128, :]
        B.op("Y", lambda e, dst=dst, srcq=srcq: e.dma_start(out=dst, in_=srcq),
             dma_inc=[(qKq[1], 16)])

    def load_v(pr):
        nonlocal v_ld
        waits = list(b_stores) + [(sT, pv_last_t), (sG, vones_ms)]
        for hh in range(2):
            h = 2 * pr + hh
            off = 0 if hh == 0 else 65
            for cq in range(4):
                dst = vbuf[:, cq * 8:(cq + 1) * 8, off:off + 64]
                srcv = vh[:, cq * 8:(cq + 1) * 8, h * 64:(h + 1) * 64]
                B.op("Y", lambda e, dst=dst, srcv=srcv: e.dma_start(out=dst, in_=srcv),
                     waits=waits, dma_inc=[(qV, 16)])
                waits = []
        v_ld = qV.n

    for pr in ([] if _SKIP_C else range(NP)):
        load_pair(pr)
        load_v(pr)
        kq_ld0 = qKq[0].n
        kq_ld1 = qKq[1].n
        kqk = kqv[:, 0, :]
        kqq = kqv[:, 1, :]

        sq_mm_t = {}
        red_v = {}
        for it in range(16):
            ch = it % 8
            srcs = (kqq if it < 8 else kqk)[:, ch * 512:(ch + 1) * 512]
            sqo = pTv[:, it % 2, 0:512]
            waits = [(qKq[1 if it < 8 else 0], kq_ld1 if it < 8 else kq_ld0)]
            if it >= 2:
                waits.append((sT, sq_mm_t[it - 2]))
            else:
                waits.append((sT, pv_last_t))
                waits.extend(b_stores)
            B.op("V", lambda e, d=sqo, s=srcs: e.tensor_mul(d, s, s),
                 waits=waits, inc=[(sV, 1)])
            sq_done = sV.n
            nb = 4 + it % 2
            waits = [(sV, sq_done), (sV, o_free_v), (sG, const_ms)]
            if it >= 2:
                waits.append((sV, red_v[it - 2]))
            p = B.op("T", lambda e, o=bank(nb)[0:1, :], s=sqo:
                     e.matmul(o, ones128[:, :], s, start=True, stop=True),
                     waits=waits, inc=[(sT, 1)])
            sq_mm_t[it] = p[sT]
            dcol = nstat[0:1, it:it + 1]
            p = B.op("V", lambda e, d=dcol, s=bank(nb)[0:1, :]:
                     e.reduce_max(d, s, axis=AX.X),
                     waits=[(sT, sq_mm_t[it])], inc=[(sV, 1)])
            red_v[it] = p[sV]
            advance_d(3)
        normsq_last_v = sV.n
        B.op("V", lambda e: e.reduce_max(nstat[0:1, 16:17], nstat[0:1, 0:8], axis=AX.X),
             inc=[(sV, 1)])
        B.op("V", lambda e: e.reduce_max(nstat[0:1, 17:18], nstat[0:1, 8:16], axis=AX.X),
             inc=[(sV, 1)])
        B.op("V", lambda e: e.tensor_add(nstat[0:1, 18:19], nstat[0:1, 16:17],
                                         nstat[0:1, 17:18]), inc=[(sV, 1)])
        B.op("V", lambda e: e.tensor_scalar(negb11[:, :], nstat[0:1, 18:19], -1.0 / 16.0,
                                            None, ALU.mult), inc=[(sV, 1)])
        nv_ = sV.n
        B.op("T", lambda e, o=psA[:, 4, 0:1]: e.matmul(o, ones1[:, :], negb11[:, :],
                                                       start=True, stop=True),
             waits=[(sV, nv_)], inc=[(sT, 1)])
        nt_ = sT.n
        p = B.op("V", lambda e, s=psA[:, 4, 0:1]: e.tensor_copy(negB[:, :], s),
                 waits=[(sT, nt_)], inc=[(sV, 1)])
        negB_v = p[sV]
        o_free_v = p[sV]
        advance_d(24)

        for w in range(NW):
            nk = 4 * (w + 1)
            pv_mm_t = {}
            smm_t = {}

            def emit_smm(ki, w=w, nk=nk, kqk=kqk, kqq=kqq, kq_ld0=kq_ld0, kq_ld1=kq_ld1):
                par = ki % 2
                waits = [(qKq[0], kq_ld0), (qKq[1], kq_ld1)]
                if exp_done.get((w, ki - 2)) is not None:
                    waits.append((sS, exp_done[(w, ki - 2)]))
                elif ki < 2 and exp_done.get("last") is not None:
                    waits.append((sS, exp_done["last"]))
                diag = ki >= 4 * w
                for hh, bx in ((0, 2 * par), (1, 2 * par + 1)):
                    r0 = hh * 64
                    lhsT = kqk[r0:r0 + 64, ki * 128:(ki + 1) * 128]
                    rhs = kqq[r0:r0 + 64, w * 512:(w + 1) * 512]
                    B.op("T", lambda e, o=bank(bx), lhsT=lhsT, rhs=rhs, diag=diag:
                         e.matmul(o, lhsT, rhs, start=True, stop=(not diag)),
                         waits=waits, inc=[(sT, 1)])
                    waits = []
                    if diag:
                        j = ki - 4 * w
                        B.op("T", lambda e, o=bank(bx), m=masksb[:, j, :]:
                             e.matmul(o, identb[:, :], m, start=False, stop=True),
                             inc=[(sT, 1)])
                smm_t[ki] = sT.n

            for ki in range(min(2, nk)):
                emit_smm(ki)
            for ki in range(nk):
                par = ki % 2
                pslot = ki % 3
                waits = [(sT, smm_t[ki]), (sV, negB_v)]
                if pT_rd.get((w, ki - 3)) is not None:
                    waits.append((sT, pT_rd[(w, ki - 3)]))
                elif ki < 3 and pT_rd.get("last") is not None:
                    waits.append((sT, pT_rd["last"]))
                p = B.op("S", lambda e, d=pTv[:, pslot, :], s=wide(2 * par):
                         e.activation(d, s, AF.Exp, bias=negB[:, 0:1], scale=0.125),
                         waits=waits, inc=[(sS, 1)])
                exp_done[(w, ki)] = p[sS]
                advance_d(4)
                if ki + 2 < nk:
                    emit_smm(ki + 2)
                waits = [(sS, exp_done[(w, ki)]), (qV, v_ld)]
                if ki == 0:
                    waits.append((sV, o_free_v))
                for hh, ob in ((0, 4), (1, 5)):
                    lhsT = vbufraw[:, ki * 130 + hh * 65: ki * 130 + hh * 65 + 128]
                    rhs = pTv[:, pslot, hh * 512:(hh + 1) * 512]
                    B.op("T", lambda e, o=bank(ob), lhsT=lhsT, rhs=rhs,
                         ki=ki, nk=nk:
                         e.matmul(o, lhsT, rhs, start=(ki == 0), stop=(ki == nk - 1)),
                         waits=waits, inc=[(sT, 1)])
                    waits = []
                pv_mm_t[ki] = sT.n
                pT_rd[(w, ki)] = sT.n
            pv_last_t = pv_mm_t[nk - 1]
            kq_last_t = pv_last_t
            pT_rd["last"] = pv_last_t
            exp_done["last"] = exp_done[(w, nk - 1)]

            for hh, ob in ((0, 4), (1, 5)):
                dst = ovst[0:65, hh * 512:(hh + 1) * 512]
                B.op("V", lambda e, dst=dst, s=bank(ob)[0:65, :]: e.tensor_copy(dst, s),
                     waits=[(sT, pv_last_t)], inc=[(sV, 1)])
            ov_evac_v = sV.n
            o_free_v = sV.n
            tb = 6 + w % 2
            tbf = bank_bf(tb)
            waits = bank_w(tb, [(sV, ov_evac_v)])
            for s8 in range(8):
                hh, bcol = s8 // 4, s8 % 4
                srcb = ovst[0:65, hh * 512 + bcol * 128: hh * 512 + (bcol + 1) * 128]
                dst = tbf[:, s8 * 66:s8 * 66 + 65]
                B.op("T", lambda e, dst=dst, srcb=srcb:
                     e.transpose(dst, srcb, identb[0:65, 0:65]),
                     waits=waits, inc=[(sT, 1)])
                waits = []
            bank_w_done(tb)
            tr_done = sT.n
            p = B.op("V", lambda e, s=tbf[:, 0:528]: e.tensor_copy(ptsb[:, 0:528], s),
                     waits=[(sT, tr_done), (sV, acc_zero_v)], inc=[(sV, 1)])
            bank_r_done(tb, sV, p[sV])
            B.op("V", lambda e, s=ptsb[:, 64:64 + 7 * 66 + 1:66]: e.reciprocal(rl[:, 0:8], s),
                 inc=[(sV, 1)])
            for s8 in range(8):
                hh, bcol = s8 // 4, s8 % 4
                h = 2 * pr + hh
                tt = 4 * w + bcol
                dsta = acc[:, tt, h * 64:(h + 1) * 64]
                srcb = ptsb[:, s8 * 66:s8 * 66 + 64]
                import os as _o
                if int(_o.environ.get("DIRECT_WRITE", "0")):
                    B.op("V", lambda e, dsta=dsta, srcb=srcb, r=rl[:, s8:s8 + 1]:
                         e.tensor_scalar(dsta, srcb, r, None, ALU.mult),
                         inc=[(sV, 1)])
                else:
                    B.op("V", lambda e, srcb=srcb, r=rl[:, s8:s8 + 1]:
                         e.tensor_scalar(ptmp[:, :], srcb, r, None, ALU.mult),
                         inc=[(sV, 1)])
                    B.op("V", lambda e, dsta=dsta: e.tensor_add(dsta, dsta, ptmp[:, :]),
                         waits=[(sG, a_end_g)], inc=[(sV, 1)])
            acc_ready = sV.n
            advance_d(6)

    tail["attn_t"] = sT.n

    while d_state["idx"] < len(dq):
        dq[d_state["idx"]][1]()
        d_state["idx"] += 1

    for tt in range(NT):
        if tt not in tail["fin_v"]:
            tail["fin_v"][tt] = sV.n
        emit_tail(tt)

    B.op("G", lambda e: e.nop(),
         waits=[(qOU[0], qOU[0].n), (qOU[1], qOU[1].n)], inc=[(sG, 1)])

    with nc.Block() as block:
        @block.tensor
        def _(eng):
            for fn in B.streams["T"]:
                fn(eng)

        @block.vector
        def _(eng):
            for fn in B.streams["V"]:
                fn(eng)

        @block.scalar
        def _(eng):
            for fn in B.streams["S"]:
                fn(eng)

        @block.sync
        def _(eng):
            for fn in B.streams["Y"]:
                fn(eng)

        @block.gpsimd
        def _(eng):
            for fn in B.streams["G"]:
                fn(eng)

    return nc


def kernel(L0_attn_x, L0_attn_W_attn, L0_attn_b_attn,
           L0_ffn_w_gate, L0_ffn_w_up, L0_ffn_w_down):
    x = np.asarray(L0_attn_x, dtype=np.float32)
    W = np.ascontiguousarray(np.asarray(L0_attn_W_attn, dtype=np.float32))
    b = np.asarray(L0_attn_b_attn, dtype=np.float32)
    wg = np.ascontiguousarray(np.asarray(L0_ffn_w_gate, dtype=np.int32))
    wu = np.ascontiguousarray(np.asarray(L0_ffn_w_up, dtype=np.int32))
    wd = np.ascontiguousarray(np.asarray(L0_ffn_w_down, dtype=np.int32))

    bpt = np.ascontiguousarray(b.reshape(24, 128).T)
    brow = np.ascontiguousarray(b[2048:3072].reshape(1, 1024))
    identb = np.eye(128, dtype=ml_dtypes.bfloat16)
    kk = np.arange(128)[:, None]
    qq = np.arange(512)[None, :]
    masks = np.stack([
        np.where(kk + 128 * j <= qq, 0.0, -1e9).astype(ml_dtypes.bfloat16)
        for j in range(4)
    ])

    nc = build_nc(debug=globals().get("_DEBUG_BUILD", False))
    in_maps = []
    for i in range(8):
        in_maps.append({
            "x": np.ascontiguousarray(x[i]),
            "w": W, "bpt_in": bpt, "brow_in": brow,
            "wg": wg, "wu": wu, "wd": wd,
            "identb_in": identb, "masks": masks,
        })
    res = run_bass_kernel_spmd(nc, in_maps, core_ids=list(range(8)))
    if globals().get("_DEBUG_BUILD", False):
        return res
    return np.stack([np.asarray(r["out"], dtype=np.float32) for r in res.results])


# version 27
# speedup vs baseline: 1.0072x; 1.0072x over previous
import numpy as np
import ml_dtypes
import concourse.bass as bass
import concourse.mybir as mybir
from concourse.bass_utils import run_bass_kernel_spmd

F32 = mybir.dt.float32
BF16 = mybir.dt.bfloat16
I32 = mybir.dt.int32
AF = mybir.ActivationFunctionType
ALU = mybir.AluOpType
AX = mybir.AxisListType

T = 4096
C = 1024
H3 = 3072
NT = T // 128
NCH = C // 128
NP = 8
NW = T // 512
NHT = H3 // 128
NGRP = 4
GHT = NHT // NGRP
INV127 = 1.0 / 127.0
INV127SQ = 1.0 / (127.0 * 127.0)

ENGS = ("T", "V", "S", "Y", "G")


class Sem:
    def __init__(self, nc, name):
        self._cm = nc.semaphore(name)
        self.h = self._cm.__enter__()
        self.n = 0


class Builder:

    def __init__(self, nc):
        self.nc = nc
        self.streams = {e: [] for e in ENGS}
        self.known = {e: {} for e in ENGS}
        self.own = {}

    def op(self, e, fn, waits=(), inc=(), dma_inc=()):
        known = self.known[e]
        waits = list(waits)
        own = self.own.get(e)
        if own is not None:
            waits.append((own, own.n))
        wl = []
        for (s, v) in waits:
            if v <= 0 or known.get(s, -1) >= v:
                continue
            wl.append((s.h, v))
            known[s] = max(known.get(s, -1), v)
        post = {}
        pairs = []
        for (s, n) in inc:
            s.n += n
            pairs.append((s.h, n))
            post[s] = s.n
        for (s, n) in dma_inc:
            s.n += n
            pairs.append((s.h, n))
            post[s] = s.n

        def run(eng, wl=wl, pairs=pairs, fn=fn):
            for (h, v) in wl:
                eng.wait_ge(h, v)
            inst = fn(eng)
            for (h, n) in pairs:
                inst = inst.then_inc(h, n)

        self.streams[e].append(run)
        return post


def build_nc(debug=False):
    nc = bass.Bass("TRN2", target_bir_lowering=False,
                   dynamic_dma_scratch_size=16384)
    ikind = "ExternalOutput" if debug else "Internal"

    x_d = nc.dram_tensor("x", [T, C], F32, kind="ExternalInput").ap()
    w_d = nc.dram_tensor("w", [C, H3], F32, kind="ExternalInput").ap()
    bpt_d = nc.dram_tensor("bpt_in", [128, 24], F32, kind="ExternalInput").ap()
    brow_d = nc.dram_tensor("brow_in", [1, 1024], F32, kind="ExternalInput").ap()
    wg_d = nc.dram_tensor("wg", [H3, C], I32, kind="ExternalInput").ap()
    wu_d = nc.dram_tensor("wu", [H3, C], I32, kind="ExternalInput").ap()
    wd_d = nc.dram_tensor("wd", [C, H3], I32, kind="ExternalInput").ap()
    identb_d = nc.dram_tensor("identb_in", [128, 128], BF16, kind="ExternalInput").ap()
    masks_d = nc.dram_tensor("masks", [4, 128, 512], BF16, kind="ExternalInput").ap()
    out_d = nc.dram_tensor("out", [T, C], F32, kind="ExternalOutput").ap()

    qTh = nc.dram_tensor("qTh", [C, T], BF16, kind=ikind).ap()
    kTh = nc.dram_tensor("kTh", [C, T], BF16, kind=ikind).ap()
    vh = nc.dram_tensor("vh", [128, NT, C], BF16, kind=ikind).ap()

    big0 = nc.alloc_sbuf_tensor("big0", [128, NCH, T], BF16).ap()
    big1 = nc.alloc_sbuf_tensor("big1", [128, 32768], BF16).ap()
    big2 = nc.alloc_sbuf_tensor("big2", [128, 15488], BF16).ap()
    hTt = nc.alloc_sbuf_tensor("hTt", [128, 3, T], BF16).ap()
    wdt = nc.alloc_sbuf_tensor("wdt", [128, 3, 1024], BF16).ap()
    xnstage = nc.alloc_sbuf_tensor("xnstage", [128, 1024], BF16).ap()
    wstage = nc.alloc_sbuf_tensor("wstage", [128, 2, 1024], BF16).ap()
    wgt_t = nc.alloc_sbuf_tensor("wgt_t", [128, 1024], BF16).ap()
    wut_t = nc.alloc_sbuf_tensor("wut_t", [128, 1024], BF16).ap()
    gs = nc.alloc_sbuf_tensor("gs", [128, 512], BF16).ap()
    masksb = nc.alloc_sbuf_tensor("masksb", [128, 4, 512], BF16).ap()
    identb = nc.alloc_sbuf_tensor("identb", [128, 128], BF16).ap()
    ones128 = nc.alloc_sbuf_tensor("ones128", [128, 1], BF16).ap()
    ones1 = nc.alloc_sbuf_tensor("ones1", [1, 128], BF16).ap()
    bpt = nc.alloc_sbuf_tensor("bpt", [128, 24], F32).ap()
    browb = nc.alloc_sbuf_tensor("browb", [1, 1024], BF16).ap()
    stats = nc.alloc_sbuf_tensor("stats", [128, 32], F32).ap()
    epsb = nc.alloc_sbuf_tensor("epsb", [128, 1], F32).ap()
    nstat = nc.alloc_sbuf_tensor("nstat", [1, 32], F32).ap()
    negb11 = nc.alloc_sbuf_tensor("negb11", [1, 1], BF16).ap()
    negB = nc.alloc_sbuf_tensor("negB", [128, 1], F32).ap()
    rl = nc.alloc_sbuf_tensor("rl", [128, 8], F32).ap()
    ptsb = nc.alloc_sbuf_tensor("ptsb", [128, 544], BF16).ap()
    ptmp = nc.alloc_sbuf_tensor("ptmp", [128, 64], BF16).ap()

    acc = big1[:, 0:32768].rearrange("p (t c) -> p t c", t=NT)
    Wv = big2[:, 0:12288].rearrange("p (c n) -> p c n", c=NCH)
    qkstage = big2[:, 12288:14336].rearrange("p (r n) -> p r n", r=4)
    kqv = big2[:, 0:8192].rearrange("p (i t) -> p i t", i=2)
    vbufraw = big2[:, 8192:12416]
    vbuf = vbufraw[:, 0:4160].rearrange("p (c n) -> p c n", c=NT)
    pTv = big2[:, 12416:15488].rearrange("p (i n) -> p i n", i=3)
    ovst = xnstage[:, 0:1024]
    smxA = hTt[:, 0:2, :].rearrange("p a t -> p (a t)").bitcast(F32) \
        .rearrange("p (i n) -> p i n", i=4)
    outstF = big2[:, 0:8192].bitcast(F32).rearrange("p (i n) -> p i n", i=4)

    psA = nc.alloc_psum_tensor("psA", [128, 8, 512], F32).ap()

    def bank(i):
        return psA[:, i, :]

    def wide(i):
        return psA[:, i:i + 2, :].rearrange("p a b -> p (a b)")

    def bank_bf(i):
        return psA[:, i, :].bitcast(BF16)

    B = Builder(nc)
    qC = Sem(nc, "qC")
    qX = [Sem(nc, "qX0"), Sem(nc, "qX1"), Sem(nc, "qX2"), Sem(nc, "qX3")]
    qKq = [Sem(nc, "qK0"), Sem(nc, "qK1")]
    qV = Sem(nc, "qV")
    qW = Sem(nc, "qW")
    qWS = [Sem(nc, "qWS0"), Sem(nc, "qWS1")]
    qST = [Sem(nc, "qS0"), Sem(nc, "qS1"), Sem(nc, "qS2"), Sem(nc, "qS3")]
    qOU = [Sem(nc, "qO%d" % i) for i in range(4)]
    sT = Sem(nc, "sT")
    sV = Sem(nc, "sV")
    sS = Sem(nc, "sS")
    sG = Sem(nc, "sG")
    B.own = {"V": sV, "S": sS, "G": sG}

    B.op("Y", lambda e: e.dma_start(out=identb[:, :], in_=identb_d[:, :]), dma_inc=[(qC, 16)])
    for j in range(4):
        dst = masksb[:, j, :]
        src = masks_d[j, :, :]
        B.op("Y", lambda e, dst=dst, src=src: e.dma_start(out=dst, in_=src), dma_inc=[(qC, 16)])
    B.op("Y", lambda e: e.dma_start(out=bpt[:, :], in_=bpt_d[:, :]), dma_inc=[(qC, 16)])
    B.op("G", lambda e: e.dma_start(out=browb[:, :], in_=brow_d[:, :]), dma_inc=[(qC, 16)])
    const_ld = qC.n
    B.op("G", lambda e: e.memset(ones128[:, :], 1.0), inc=[(sG, 1)])
    B.op("G", lambda e: e.memset(ones1[:, :], 1.0), inc=[(sG, 1)])
    B.op("G", lambda e: e.memset(epsb[:, :], 1e-5), inc=[(sG, 1)])
    const_ms = sG.n


    xnslot = [xnstage[:, 0:1024], wstage[:, 1, :]]
    a_apply_v = [0] * NT
    a_sq_s = [0] * NT
    a_tr_t = [0] * NT
    a_evac_v = [0] * NT
    a_xld = [0] * NT

    a_copy_g = [0] * NT
    a_sqrt_s = [0] * NT

    def a_stage1(t):
        sl = t % 4
        o = sl * 8
        xs = smxA[:, sl, :]
        xsrc = x_d[t * 128:(t + 1) * 128, :]
        waits = []
        if t >= 4:
            waits = [(sV, a_apply_v[t - 4]), (sG, a_copy_g[t - 4]),
                     (sS, a_sq_s[t - 4])]
        B.op("Y", lambda e, xs=xs, xsrc=xsrc: e.dma_start(out=xs, in_=xsrc),
             waits=waits, dma_inc=[(qX[sl], 16)])
        a_xld[t] = qX[sl].n

        p = B.op("G", lambda e, d=acc[:, t, :], xs=xs: e.tensor_copy(d, xs),
                 waits=[(qX[sl], a_xld[t])], inc=[(sG, 1)])
        a_copy_g[t] = p[sG]

        p = B.op("S", lambda e, xs=xs, d=pTv[:, 0, :], a=stats[:, o + 2:o + 3]:
                 e.activation(d, xs, AF.Square, accum_out=a),
                 waits=[(qX[sl], a_xld[t])], inc=[(sS, 1)])
        a_sq_s[t] = p[sS]
        B.op("V", lambda e, xs=xs, d=stats[:, o + 0:o + 1]: e.reduce_sum(d, xs, axis=AX.X),
             waits=[(qX[sl], a_xld[t])], inc=[(sV, 1)])
        B.op("V", lambda e, s=stats[:, o + 0:o + 1], d=stats[:, o + 1:o + 2]:
             e.tensor_scalar(d, s, 1.0 / C, None, ALU.mult), inc=[(sV, 1)])
        B.op("V", lambda e, m=stats[:, o + 1:o + 2], d=stats[:, o + 3:o + 4]:
             e.tensor_mul(d, m, m), inc=[(sV, 1)])
        B.op("V", lambda e, ssq=stats[:, o + 2:o + 3], msq=stats[:, o + 3:o + 4],
             d=stats[:, o + 4:o + 5]:
             e.tensor_scalar(d, ssq, 1.0 / C, msq, ALU.mult, ALU.subtract),
             waits=[(sS, a_sq_s[t])], inc=[(sV, 1)])
        pv = sV.n
        p = B.op("S", lambda e, v=stats[:, o + 4:o + 5], d=stats[:, o + 5:o + 6]:
                 e.activation(d, v, AF.Sqrt, bias=epsb[:, 0:1]),
                 waits=[(sV, pv), (sG, const_ms)], inc=[(sS, 1)])
        a_sqrt_s[t] = p[sS]

    def a_stage2(t):
        sl = t % 4
        o = sl * 8
        xs = smxA[:, sl, :]
        B.op("V", lambda e, s=stats[:, o + 5:o + 6], d=stats[:, o + 6:o + 7]:
             e.reciprocal(d, s), waits=[(sS, a_sqrt_s[t])], inc=[(sV, 1)])
        waits = []
        if t >= 2:
            waits.append((sT, a_tr_t[t - 2]))
        p = B.op("V", lambda e, xs=xs, m=stats[:, o + 1:o + 2], r=stats[:, o + 6:o + 7],
                 d=xnslot[t % 2]:
                 e.tensor_scalar(d, xs, m, r, ALU.subtract, ALU.mult),
                 waits=waits, inc=[(sV, 1)])
        a_apply_v[t] = p[sV]
        tb = bank_bf(6 + t % 2)
        waits = [(sV, a_apply_v[t]), (qC, const_ld)]
        if t >= 2:
            waits.append((sV, a_evac_v[t - 2]))
        for c in range(NCH):
            dst = tb[:, c * 128:(c + 1) * 128]
            src = xnslot[t % 2][:, c * 128:(c + 1) * 128]
            p = B.op("T", lambda e, dst=dst, src=src: e.transpose(dst, src, identb[:, :]),
                     waits=waits, inc=[(sT, 1)])
            waits = []
        a_tr_t[t] = p[sT]
        dstx = big0[:, :, t * 128:(t + 1) * 128]
        srcx = tb.rearrange("p (c w) -> p c w", c=NCH)
        p = B.op("V", lambda e, dstx=dstx, srcx=srcx: e.tensor_copy(dstx, srcx),
                 waits=[(sT, a_tr_t[t])], inc=[(sV, 1)])
        a_evac_v[t] = p[sV]

    for t in range(NT):
        a_stage1(t)
        if t >= 1:
            a_stage2(t - 1)
    a_stage2(NT - 1)
    a_end_v = sV.n
    a_end_g = sG.n
    a_end_s = sS.n
    if debug:
        for c in range(NCH):
            dst = xnTd[c]
            src = big0[:, c, :]
            B.op("S", lambda e, dst=dst, src=src: e.dma_start(out=dst, in_=src),
                 waits=[(sV, a_end_v)], dma_inc=[(qC, 16)])

    ft_store_done = {}
    b_cast = [0, 0]
    qk_bank_evac = {}
    stage_dma = [0, 0, 0, 0]
    bmm_i = 0

    for half in range(2):
        waits = [] if half == 0 else [(sT, sT.n)]
        for c in range(NCH):
            dst = Wv[:, c, :]
            src = w_d[c * 128:(c + 1) * 128, half * 1536:(half + 1) * 1536]
            B.op("G", lambda e, dst=dst, src=src: e.dma_start(out=dst, in_=src),
                 waits=waits, dma_inc=[(qW, 16)])
            waits = []
        b_cast[half] = qW.n

        qk_tiles = list(range(12)) if half == 0 else list(range(4))
        for fl in qk_tiles:
            ft = half * 12 + fl
            for tw in range(NW):
                bb2 = bmm_i % 2
                bmm_i += 1
                for c in range(NCH):
                    lhsT = Wv[:, c, fl * 128:(fl + 1) * 128]
                    rhs = big0[:, c, tw * 512:(tw + 1) * 512]
                    waits = [(qW, b_cast[half]), (sV, a_end_v)]
                    if c == 0 and bb2 in qk_bank_evac:
                        waits.append((sV, qk_bank_evac[bb2]))
                    B.op("T", lambda e, lhsT=lhsT, rhs=rhs, o=bank(bb2), c=c:
                         e.matmul(o, lhsT, rhs, start=(c == 0), stop=(c == NCH - 1)),
                         waits=waits, inc=[(sT, 1)])
                mm_done = sT.n
                r = (ft * NW + tw) % 4
                p = B.op("V", lambda e, d=qkstage[:, r, :], s=bank(bb2), bc=bpt[:, ft:ft + 1]:
                         e.tensor_scalar(d, s, bc, None, ALU.add),
                         waits=[(sT, mm_done), (qC, const_ld), (qST[r], stage_dma[r])],
                         inc=[(sV, 1)])
                qk_bank_evac[bb2] = p[sV]
                ev = p[sV]
                if ft < 8:
                    dst = qTh[ft * 128:(ft + 1) * 128, tw * 512:(tw + 1) * 512]
                else:
                    dst = kTh[(ft - 8) * 128:(ft - 8 + 1) * 128, tw * 512:(tw + 1) * 512]
                p = B.op("S", lambda e, dst=dst, s=qkstage[:, r, :]: e.dma_start(out=dst, in_=s),
                         waits=[(sV, ev)], dma_inc=[(qST[r], 16)])
                stage_dma[r] = p[qST[r]]
                if tw == NW - 1:
                    ft_store_done[ft] = [(qST[rr], qST[rr].n) for rr in range(4)]

    v_bank_evac = {}
    for tt in range(NT):
        for vc in range(2):
            bb2 = 2 + (tt * 2 + vc) % 2
            for c in range(NCH):
                lhsT = big0[:, c, tt * 128:(tt + 1) * 128]
                rhs = Wv[:, c, 512 + vc * 512:512 + (vc + 1) * 512]
                waits = [(qW, b_cast[1])]
                if c == 0 and bb2 in v_bank_evac:
                    waits.append((sV, v_bank_evac[bb2]))
                B.op("T", lambda e, lhsT=lhsT, rhs=rhs, o=bank(bb2), c=c:
                     e.matmul(o, lhsT, rhs, start=(c == 0), stop=False),
                     waits=waits, inc=[(sT, 1)])
            B.op("T", lambda e, o=bank(bb2), r=browb[0:1, vc * 512:(vc + 1) * 512]:
                 e.matmul(o, ones1[:, :], r, start=False, stop=True),
                 waits=[(qC, const_ld), (sG, const_ms)], inc=[(sT, 1)])
            mm_done = sT.n
            r = (tt * 2 + vc) % 4
            p = B.op("V", lambda e, d=qkstage[:, r, :], s=bank(bb2):
                     e.tensor_copy(d, s),
                     waits=[(sT, mm_done), (qST[r], stage_dma[r])], inc=[(sV, 1)])
            v_bank_evac[bb2] = p[sV]
            ev = p[sV]
            dst = vh[:, tt, vc * 512:(vc + 1) * 512]
            p = B.op("S", lambda e, dst=dst, s=qkstage[:, r, :]: e.dma_start(out=dst, in_=s),
                     waits=[(sV, ev)], dma_inc=[(qST[r], 16)])
            stage_dma[r] = p[qST[r]]
    b_end_t = sT.n
    b_end_v = sV.n
    b_stores = [(qST[r], qST[r].n) for r in range(4)]

    acc_zero_v = a_end_v

    B.op("G", lambda e: e.memset(vbuf[:, :, 64:65], 1.0),
         waits=[(sT, b_end_t)], inc=[(sG, 1)])
    B.op("G", lambda e: e.memset(vbuf[:, :, 129:130], 1.0), inc=[(sG, 1)])
    B.op("G", lambda e: e.memset(vbufraw[:, 4160:4224], 0.0), inc=[(sG, 1)])
    vones_ms = sG.n

    b67 = {6: {"w": b_end_t, "r": []}, 7: {"w": b_end_t, "r": []}}

    def bank_w(b, waits=()):
        return list(waits) + b67[b]["r"]

    def bank_w_done(b):
        b67[b]["w"] = sT.n
        b67[b]["r"] = []

    def bank_r(b, waits=()):
        return list(waits) + [(sT, b67[b]["w"])]

    def bank_r_done(b, s, v):
        b67[b]["r"].append((s, v))

    ds = {
        "wstage_tr": [b_end_t, b_end_t],
        "wgt_ev": acc_zero_v, "wut_ev": acc_zero_v,
        "silu": 0, "hmul": 0, "uev": 0,
        "gu_done": b_end_t,
        "down_t": b_end_t,
        "h_ready": 0, "wdt_ready": 0,
        "stt_last": acc_zero_v,
    }
    DG = 8
    DGH = 3
    wd_r = wd_d.rearrange("(dc p) h -> p dc h", p=128)

    def d_jit_gu(ht):
        for (slot, wsrc, dstt, evk) in ((0, wg_d, wgt_t, "wgt_ev"), (1, wu_d, wut_t, "wut_ev")):
            waits = [(sT, ds["wstage_tr"][slot])]
            dst = wstage[:, slot, :]
            srcw = wsrc[ht * 128:(ht + 1) * 128, :]
            B.op("G", lambda e, dst=dst, srcw=srcw: e.dma_start(out=dst, in_=srcw),
                 waits=waits, dma_inc=[(qWS[slot], 16)])
            cast = qWS[slot].n
            tb = 6 + slot
            tbf = bank_bf(tb)
            waits = bank_w(tb, [(qWS[slot], cast), (sV, ds[evk])])
            for dc in range(NCH):
                dstp = tbf[:, dc * 128:(dc + 1) * 128]
                srcp = wstage[:, slot, dc * 128:(dc + 1) * 128]
                B.op("T", lambda e, dstp=dstp, srcp=srcp:
                     e.transpose(dstp, srcp, identb[:, :]),
                     waits=waits, inc=[(sT, 1)])
                waits = []
            ds["wstage_tr"][slot] = sT.n
            bank_w_done(tb)
            trd = sT.n
            p = B.op("V", lambda e, d=dstt, s=tbf: e.tensor_copy(d, s),
                     waits=[(sT, trd)], inc=[(sV, 1)])
            bank_r_done(tb, sV, p[sV])
            ds[evk] = p[sV]

    def d_guh(g, j3, tw):
        gb = 6 + (tw % 2)
        ub = 6 + ((tw + 1) % 2)
        for (tb, wt, evk) in ((gb, wgt_t, "wgt_ev"), (ub, wut_t, "wut_ev")):
            waits = bank_w(tb, [(sV, ds[evk])])
            if tw == 0 and j3 == 0:
                waits.append((sT, ds["down_t"]))
            for c in range(NCH):
                lhsT = wt[:, c * 128:(c + 1) * 128]
                rhs = big0[:, c, tw * 512:(tw + 1) * 512]
                B.op("T", lambda e, o=bank(tb), lhsT=lhsT, rhs=rhs, c=c:
                     e.matmul(o, lhsT, rhs, start=(c == 0), stop=(c == NCH - 1)),
                     waits=waits, inc=[(sT, 1)])
                waits = []
            bank_w_done(tb)
        ds["gu_done"] = sT.n
        gu_done = sT.n
        waits = bank_r(gb, [(sV, ds["hmul"])])
        p = B.op("S", lambda e, s=bank(gb): e.activation(gs[:, :], s, AF.Silu,
                                                         scale=INV127),
                 waits=waits, inc=[(sS, 1)])
        ds["silu"] = p[sS]
        bank_r_done(gb, sS, p[sS])
        dsth = hTt[:, j3, tw * 512:(tw + 1) * 512]
        waits = bank_r(ub, [(sS, ds["silu"]), (sG, a_end_g), (sS, a_end_s)])
        p = B.op("V", lambda e, d=dsth, s=bank(ub): e.tensor_tensor(d, gs[:, :], s, ALU.mult),
                 waits=waits, inc=[(sV, 1)])
        ds["hmul"] = p[sV]
        bank_r_done(ub, sV, p[sV])
        if j3 == DGH - 1 and tw == NW - 1:
            ds["h_ready"] = p[sV]

    def d_wd_jit(g, j3):
        hc = g * DGH + j3
        slot = j3 % 2
        waits = [(sT, ds["wstage_tr"][slot])]
        for dcc in range(NCH):
            dst = wstage[:, slot, dcc * 128:(dcc + 1) * 128]
            srcw = wd_r[:, dcc, hc * 128:(hc + 1) * 128]
            B.op("G", lambda e, dst=dst, srcw=srcw: e.dma_start(out=dst, in_=srcw),
                 waits=waits, dma_inc=[(qWS[slot], 16)])
            waits = []
        cast = qWS[slot].n
        tb = 6 + slot
        tbf = bank_bf(tb)
        waits = bank_w(tb, [(qWS[slot], cast), (sT, ds["down_t"])])
        for dc in range(NCH):
            dstp = tbf[:, dc * 128:(dc + 1) * 128]
            srcp = wstage[:, slot, dc * 128:(dc + 1) * 128]
            B.op("T", lambda e, dstp=dstp, srcp=srcp:
                 e.transpose(dstp, srcp, identb[:, :]),
                 waits=waits, inc=[(sT, 1)])
            waits = []
        ds["wstage_tr"][slot] = sT.n
        bank_w_done(tb)
        trd = sT.n
        p = B.op("V", lambda e, d=wdt[:, j3, :], s=tbf: e.tensor_copy(d, s),
                 waits=[(sT, trd)], inc=[(sV, 1)])
        bank_r_done(tb, sV, p[sV])
        if j3 == DGH - 1:
            ds["wdt_ready"] = p[sV]

    def d_down(g, tt, last_grp_tt):
        for dc in range(2):
            tb = 6 + dc
            waits = bank_w(tb, [(sV, ds["wdt_ready"]), (sV, ds["h_ready"])])
            for j3 in range(DGH):
                lhsT = hTt[:, j3, tt * 128:(tt + 1) * 128]
                rhs = wdt[:, j3, dc * 512:(dc + 1) * 512]
                B.op("T", lambda e, o=bank(tb), lhsT=lhsT, rhs=rhs, j3=j3:
                     e.matmul(o, lhsT, rhs, start=(j3 == 0), stop=(j3 == DGH - 1)),
                     waits=waits, inc=[(sT, 1)])
                waits = []
            bank_w_done(tb)
            dd = sT.n
            dstp = acc[:, tt, dc * 512:(dc + 1) * 512]
            p = B.op("V", lambda e, d=dstp, s=bank(tb):
                     e.scalar_tensor_tensor(d, s, INV127SQ, d, ALU.mult, ALU.add),
                     waits=[(sT, dd), (sG, a_end_g)], inc=[(sV, 1)])
            ds["stt_last"] = p[sV]
            bank_r_done(tb, sV, p[sV])
        if last_grp_tt:
            ds["down_t"] = sT.n
        if g == DG - 1:
            tail["fin_v"][tt] = sV.n

    tail = {"attn_t": None, "fin_v": {},
            "add_v": {}, "outst_dma": [q.n for q in qOU]}

    def emit_tail(tt):
        sl = tt % 4
        p = B.op("V", lambda e, d=outstF[:, sl, :], b=acc[:, tt, :]:
                 e.tensor_copy(d, b),
                 waits=[(sT, tail["attn_t"]),
                        (sV, tail["fin_v"][tt]), (qOU[sl], tail["outst_dma"][sl])],
                 inc=[(sV, 1)])
        tail["add_v"][tt] = p[sV]
        dsto = out_d[tt * 128:(tt + 1) * 128, :]
        eng = "S" if tt % 2 == 0 else "Y"
        p = B.op(eng, lambda e, dsto=dsto, s=outstF[:, sl, :]: e.dma_start(out=dsto, in_=s),
                 waits=[(sV, tail["add_v"][tt])], dma_inc=[(qOU[sl], 16)])
        tail["outst_dma"][sl] = p[qOU[sl]]

    dq = []
    for g in range(DG):
        for j3 in range(DGH):
            ht = g * DGH + j3
            dq.append((10, lambda ht=ht: d_jit_gu(ht)))
            for tw in range(NW):
                dq.append((16, lambda g=g, j3=j3, tw=tw: d_guh(g, j3, tw)))
        for j3 in range(DGH):
            dq.append((9, lambda g=g, j3=j3: d_wd_jit(g, j3)))
        for tt in range(NT):
            dq.append((6, lambda g=g, tt=tt: d_down(g, tt, tt == NT - 1)))

    d_total = sum(c for c, _ in dq)
    c_total = NP * sum(4 * (w + 1) for w in range(NW)) * 4
    d_state = {"credit": 0.0, "idx": 0}
    import os
    _NO_ILV = bool(int(os.environ.get("NO_ILV", "0")))
    _SKIP_C = bool(int(os.environ.get("SKIP_C", "0")))
    _SKIP_D = bool(int(os.environ.get("SKIP_D", "0")))
    if _SKIP_D:
        dq = []

    def advance_d(units):
        if _NO_ILV:
            return
        d_state["credit"] += units * (d_total / c_total)
        while d_state["idx"] < len(dq) and d_state["credit"] >= dq[d_state["idx"]][0]:
            d_state["credit"] -= dq[d_state["idx"]][0]
            dq[d_state["idx"]][1]()
            d_state["idx"] += 1

    kq_ld = 0
    kq_last_t = b_end_t
    normsq_last_v = acc_zero_v
    v_ld = 0
    pv_last_t = b_end_t
    exp_done = {}
    pT_rd = {}
    o_free_v = b_end_v
    ov_evac_v = b_end_v
    acc_ready = acc_zero_v

    def load_pair(pr):
        waits = list(ft_store_done[pr]) + list(ft_store_done[8 + pr]) + \
                [(sT, kq_last_t), (sV, normsq_last_v)]
        dst = kqv[:, 0, :]
        srck = kTh[pr * 128:(pr + 1) * 128, :]
        B.op("Y", lambda e, dst=dst, srck=srck: e.dma_start(out=dst, in_=srck),
             waits=waits, dma_inc=[(qKq[0], 16)])
        dst = kqv[:, 1, :]
        srcq = qTh[pr * 128:(pr + 1) * 128, :]
        B.op("Y", lambda e, dst=dst, srcq=srcq: e.dma_start(out=dst, in_=srcq),
             dma_inc=[(qKq[1], 16)])

    def load_v(pr):
        nonlocal v_ld
        waits = list(b_stores) + [(sT, pv_last_t), (sG, vones_ms)]
        for hh in range(2):
            h = 2 * pr + hh
            off = 0 if hh == 0 else 65
            for cq in range(4):
                dst = vbuf[:, cq * 8:(cq + 1) * 8, off:off + 64]
                srcv = vh[:, cq * 8:(cq + 1) * 8, h * 64:(h + 1) * 64]
                B.op("Y", lambda e, dst=dst, srcv=srcv: e.dma_start(out=dst, in_=srcv),
                     waits=waits, dma_inc=[(qV, 16)])
                waits = []
        v_ld = qV.n

    for pr in ([] if _SKIP_C else range(NP)):
        load_pair(pr)
        load_v(pr)
        kq_ld0 = qKq[0].n
        kq_ld1 = qKq[1].n
        kqk = kqv[:, 0, :]
        kqq = kqv[:, 1, :]

        sq_mm_t = {}
        red_v = {}
        for it in range(16):
            ch = it % 8
            srcs = (kqq if it < 8 else kqk)[:, ch * 512:(ch + 1) * 512]
            sqo = pTv[:, it % 2, 0:512]
            waits = [(qKq[1 if it < 8 else 0], kq_ld1 if it < 8 else kq_ld0)]
            if it >= 2:
                waits.append((sT, sq_mm_t[it - 2]))
            else:
                waits.append((sT, pv_last_t))
                waits.extend(b_stores)
            B.op("V", lambda e, d=sqo, s=srcs: e.tensor_mul(d, s, s),
                 waits=waits, inc=[(sV, 1)])
            sq_done = sV.n
            nb = 4 + it % 2
            waits = [(sV, sq_done), (sV, o_free_v), (sG, const_ms)]
            if it >= 2:
                waits.append((sV, red_v[it - 2]))
            p = B.op("T", lambda e, o=bank(nb)[0:1, :], s=sqo:
                     e.matmul(o, ones128[:, :], s, start=True, stop=True),
                     waits=waits, inc=[(sT, 1)])
            sq_mm_t[it] = p[sT]
            dcol = nstat[0:1, it:it + 1]
            p = B.op("V", lambda e, d=dcol, s=bank(nb)[0:1, :]:
                     e.reduce_max(d, s, axis=AX.X),
                     waits=[(sT, sq_mm_t[it])], inc=[(sV, 1)])
            red_v[it] = p[sV]
            advance_d(3)
        normsq_last_v = sV.n
        B.op("V", lambda e: e.reduce_max(nstat[0:1, 16:17], nstat[0:1, 0:8], axis=AX.X),
             inc=[(sV, 1)])
        B.op("V", lambda e: e.reduce_max(nstat[0:1, 17:18], nstat[0:1, 8:16], axis=AX.X),
             inc=[(sV, 1)])
        B.op("V", lambda e: e.tensor_add(nstat[0:1, 18:19], nstat[0:1, 16:17],
                                         nstat[0:1, 17:18]), inc=[(sV, 1)])
        B.op("V", lambda e: e.tensor_scalar(negb11[:, :], nstat[0:1, 18:19], -1.0 / 16.0,
                                            None, ALU.mult), inc=[(sV, 1)])
        nv_ = sV.n
        B.op("T", lambda e, o=psA[:, 4, 0:1]: e.matmul(o, ones1[:, :], negb11[:, :],
                                                       start=True, stop=True),
             waits=[(sV, nv_)], inc=[(sT, 1)])
        nt_ = sT.n
        p = B.op("V", lambda e, s=psA[:, 4, 0:1]: e.tensor_copy(negB[:, :], s),
                 waits=[(sT, nt_)], inc=[(sV, 1)])
        negB_v = p[sV]
        o_free_v = p[sV]
        advance_d(24)

        for w in range(NW):
            nk = 4 * (w + 1)
            pv_mm_t = {}
            smm_t = {}

            def emit_smm(ki, w=w, nk=nk, kqk=kqk, kqq=kqq, kq_ld0=kq_ld0, kq_ld1=kq_ld1):
                par = ki % 2
                waits = [(qKq[0], kq_ld0), (qKq[1], kq_ld1)]
                if exp_done.get((w, ki - 2)) is not None:
                    waits.append((sS, exp_done[(w, ki - 2)]))
                elif ki < 2 and exp_done.get("last") is not None:
                    waits.append((sS, exp_done["last"]))
                diag = ki >= 4 * w
                for hh, bx in ((0, 2 * par), (1, 2 * par + 1)):
                    r0 = hh * 64
                    lhsT = kqk[r0:r0 + 64, ki * 128:(ki + 1) * 128]
                    rhs = kqq[r0:r0 + 64, w * 512:(w + 1) * 512]
                    B.op("T", lambda e, o=bank(bx), lhsT=lhsT, rhs=rhs, diag=diag:
                         e.matmul(o, lhsT, rhs, start=True, stop=(not diag)),
                         waits=waits, inc=[(sT, 1)])
                    waits = []
                    if diag:
                        j = ki - 4 * w
                        B.op("T", lambda e, o=bank(bx), m=masksb[:, j, :]:
                             e.matmul(o, identb[:, :], m, start=False, stop=True),
                             inc=[(sT, 1)])
                smm_t[ki] = sT.n

            for ki in range(min(2, nk)):
                emit_smm(ki)
            for ki in range(nk):
                par = ki % 2
                pslot = ki % 3
                waits = [(sT, smm_t[ki]), (sV, negB_v)]
                if pT_rd.get((w, ki - 3)) is not None:
                    waits.append((sT, pT_rd[(w, ki - 3)]))
                elif ki < 3 and pT_rd.get("last") is not None:
                    waits.append((sT, pT_rd["last"]))
                p = B.op("S", lambda e, d=pTv[:, pslot, :], s=wide(2 * par):
                         e.activation(d, s, AF.Exp, bias=negB[:, 0:1], scale=0.125),
                         waits=waits, inc=[(sS, 1)])
                exp_done[(w, ki)] = p[sS]
                advance_d(4)
                if ki + 2 < nk:
                    emit_smm(ki + 2)
                waits = [(sS, exp_done[(w, ki)]), (qV, v_ld)]
                if ki == 0:
                    waits.append((sV, o_free_v))
                for hh, ob in ((0, 4), (1, 5)):
                    lhsT = vbufraw[:, ki * 130 + hh * 65: ki * 130 + hh * 65 + 128]
                    rhs = pTv[:, pslot, hh * 512:(hh + 1) * 512]
                    B.op("T", lambda e, o=bank(ob), lhsT=lhsT, rhs=rhs,
                         ki=ki, nk=nk:
                         e.matmul(o, lhsT, rhs, start=(ki == 0), stop=(ki == nk - 1)),
                         waits=waits, inc=[(sT, 1)])
                    waits = []
                pv_mm_t[ki] = sT.n
                pT_rd[(w, ki)] = sT.n
            pv_last_t = pv_mm_t[nk - 1]
            kq_last_t = pv_last_t
            pT_rd["last"] = pv_last_t
            exp_done["last"] = exp_done[(w, nk - 1)]

            for hh, ob in ((0, 4), (1, 5)):
                dst = ovst[0:65, hh * 512:(hh + 1) * 512]
                B.op("V", lambda e, dst=dst, s=bank(ob)[0:65, :]: e.tensor_copy(dst, s),
                     waits=[(sT, pv_last_t)], inc=[(sV, 1)])
            ov_evac_v = sV.n
            o_free_v = sV.n
            tb = 6 + w % 2
            tbf = bank_bf(tb)
            waits = bank_w(tb, [(sV, ov_evac_v)])
            for s8 in range(8):
                hh, bcol = s8 // 4, s8 % 4
                srcb = ovst[0:65, hh * 512 + bcol * 128: hh * 512 + (bcol + 1) * 128]
                dst = tbf[:, s8 * 66:s8 * 66 + 65]
                B.op("T", lambda e, dst=dst, srcb=srcb:
                     e.transpose(dst, srcb, identb[0:65, 0:65]),
                     waits=waits, inc=[(sT, 1)])
                waits = []
            bank_w_done(tb)
            tr_done = sT.n
            p = B.op("V", lambda e, s=tbf[:, 0:528]: e.tensor_copy(ptsb[:, 0:528], s),
                     waits=[(sT, tr_done), (sV, acc_zero_v)], inc=[(sV, 1)])
            bank_r_done(tb, sV, p[sV])
            B.op("V", lambda e, s=ptsb[:, 64:64 + 7 * 66 + 1:66]: e.reciprocal(rl[:, 0:8], s),
                 inc=[(sV, 1)])
            for s8 in range(8):
                hh, bcol = s8 // 4, s8 % 4
                h = 2 * pr + hh
                tt = 4 * w + bcol
                dsta = acc[:, tt, h * 64:(h + 1) * 64]
                srcb = ptsb[:, s8 * 66:s8 * 66 + 64]
                import os as _o
                if int(_o.environ.get("DIRECT_WRITE", "0")):
                    B.op("V", lambda e, dsta=dsta, srcb=srcb, r=rl[:, s8:s8 + 1]:
                         e.tensor_scalar(dsta, srcb, r, None, ALU.mult),
                         inc=[(sV, 1)])
                else:
                    B.op("V", lambda e, srcb=srcb, r=rl[:, s8:s8 + 1]:
                         e.tensor_scalar(ptmp[:, :], srcb, r, None, ALU.mult),
                         inc=[(sV, 1)])
                    B.op("V", lambda e, dsta=dsta: e.tensor_add(dsta, dsta, ptmp[:, :]),
                         waits=[(sG, a_end_g)], inc=[(sV, 1)])
            acc_ready = sV.n
            advance_d(6)

    tail["attn_t"] = sT.n

    while d_state["idx"] < len(dq):
        dq[d_state["idx"]][1]()
        d_state["idx"] += 1

    for tt in range(NT):
        if tt not in tail["fin_v"]:
            tail["fin_v"][tt] = sV.n
        emit_tail(tt)

    B.op("G", lambda e: e.nop(),
         waits=[(q, q.n) for q in qOU], inc=[(sG, 1)])

    with nc.Block() as block:
        @block.tensor
        def _(eng):
            for fn in B.streams["T"]:
                fn(eng)

        @block.vector
        def _(eng):
            for fn in B.streams["V"]:
                fn(eng)

        @block.scalar
        def _(eng):
            for fn in B.streams["S"]:
                fn(eng)

        @block.sync
        def _(eng):
            for fn in B.streams["Y"]:
                fn(eng)

        @block.gpsimd
        def _(eng):
            for fn in B.streams["G"]:
                fn(eng)

    return nc


def kernel(L0_attn_x, L0_attn_W_attn, L0_attn_b_attn,
           L0_ffn_w_gate, L0_ffn_w_up, L0_ffn_w_down):
    x = np.asarray(L0_attn_x, dtype=np.float32)
    W = np.ascontiguousarray(np.asarray(L0_attn_W_attn, dtype=np.float32))
    b = np.asarray(L0_attn_b_attn, dtype=np.float32)
    wg = np.ascontiguousarray(np.asarray(L0_ffn_w_gate, dtype=np.int32))
    wu = np.ascontiguousarray(np.asarray(L0_ffn_w_up, dtype=np.int32))
    wd = np.ascontiguousarray(np.asarray(L0_ffn_w_down, dtype=np.int32))

    bpt = np.ascontiguousarray(b.reshape(24, 128).T)
    brow = np.ascontiguousarray(b[2048:3072].reshape(1, 1024))
    identb = np.eye(128, dtype=ml_dtypes.bfloat16)
    kk = np.arange(128)[:, None]
    qq = np.arange(512)[None, :]
    masks = np.stack([
        np.where(kk + 128 * j <= qq, 0.0, -1e9).astype(ml_dtypes.bfloat16)
        for j in range(4)
    ])

    nc = build_nc(debug=globals().get("_DEBUG_BUILD", False))
    in_maps = []
    for i in range(8):
        in_maps.append({
            "x": np.ascontiguousarray(x[i]),
            "w": W, "bpt_in": bpt, "brow_in": brow,
            "wg": wg, "wu": wu, "wd": wd,
            "identb_in": identb, "masks": masks,
        })
    res = run_bass_kernel_spmd(nc, in_maps, core_ids=list(range(8)))
    if globals().get("_DEBUG_BUILD", False):
        return res
    return np.stack([np.asarray(r["out"], dtype=np.float32) for r in res.results])
